# version 02 (final)
# baseline (speedup 1.0000x reference)
import numpy as np
import concourse.bass as bass
import concourse.mybir as mybir
from concourse.bass_utils import run_bass_kernel_spmd

F32 = mybir.dt.float32
BF16 = mybir.dt.bfloat16
AF = mybir.ActivationFunctionType
ALU = mybir.AluOpType
AX = mybir.AxisListType

B_, T, D = 8, 2048, 2048
DBG_P3 = False
DI, DXB, DST, DTR = 4096, 1024, 16, 128
INTER = 8192
GC = 256
E, FE, FS = 8, 1408, 5632
EPS = 1e-5
NQ = 4
TQ = T // NQ
NOC = 81
ROWS_IP = 2 * DI + 2 * DXB + DTR

OC_LIST = ([("z", c, h) for c in range(16) for h in range(2)]
           + [("xv", c2, 0) for c2 in range(8)]
           + [("bm", n2, 0) for n2 in range(8)]
           + [("cm", n, h) for n in range(16) for h in range(2)]
           + [("dt", 0, 0)])


STRICT_KEYS = {"sm", "cst", "rps", "sig", "C", "carry"}


class _Op:
    __slots__ = ("engine", "emit", "deps", "dma", "needs_sig", "sig", "tag",
                 "strict")

    def __init__(self, engine, emit, dma, tag):
        self.engine, self.emit, self.dma, self.tag = engine, emit, dma, tag
        self.deps = []
        self.needs_sig = False
        self.sig = None
        self.strict = False


class Plan:
    ENGINES = ("pe", "act", "dve", "pool", "sp")

    def __init__(self):
        self.ops = []
        self.writer = {}
        self.readers = {}

    def add(self, engine, emit, reads=(), writes=(), dma=False, tag=""):
        op = _Op(engine, emit, dma, tag)
        op.strict = any(isinstance(k, tuple) and k and k[0] in STRICT_KEYS
                        for k in writes)
        dep = {}
        for k in reads:
            w = self.writer.get(k)
            if w is not None:
                dep[id(w)] = w
        for k in writes:
            w = self.writer.get(k)
            if w is not None:
                dep[id(w)] = w
            for r in self.readers.get(k, ()):
                dep[id(r)] = r
        op.deps = list(dep.values())
        for k in writes:
            self.writer[k] = op
            self.readers[k] = []
        for k in reads:
            self.readers.setdefault(k, []).append(op)
        self.ops.append(op)
        return op

    def fence(self, engine, keys, tag="fence"):
        return self.add(engine, None, reads=(), writes=tuple(keys), tag=tag)

    def emit(self, nc):
        for op in self.ops:
            for d in op.deps:
                if d.engine != op.engine or d.dma or op.dma or d.strict:
                    d.needs_sig = True
        eng_cnt = {e: 0 for e in self.ENGINES}
        dma_cnt = {e: 0 for e in self.ENGINES}
        sems = {}
        for e in self.ENGINES:
            sems["e" + e] = nc.alloc_semaphore("sem_" + e)
            sems["d" + e] = nc.alloc_semaphore("dsem_" + e)
        for op in self.ops:
            if op.dma:
                dma_cnt[op.engine] += 16
                op.sig = ("d" + op.engine, dma_cnt[op.engine])
            elif op.needs_sig:
                eng_cnt[op.engine] += 1
                op.sig = ("e" + op.engine, eng_cnt[op.engine])
        per_eng = {e: [] for e in self.ENGINES}
        for op in self.ops:
            per_eng[op.engine].append(op)

        def replay(eng_obj, ename):
            marks = {}
            nwait = 0
            for op in per_eng[ename]:
                need = {}
                for dd in op.deps:
                    if dd.sig is None:
                        continue
                    if (dd.engine == ename and not dd.dma and not op.dma
                            and not dd.strict):
                        continue
                    sname, val = dd.sig
                    if val > need.get(sname, 0):
                        need[sname] = val
                for sname, val in need.items():
                    if marks.get(sname, 0) >= val:
                        continue
                    eng_obj.wait_ge(sems[sname], val)
                    marks[sname] = val
                    nwait += 1
                if op.emit is None:
                    inst = (eng_obj.engine_nop() if hasattr(eng_obj, "engine_nop")
                            else eng_obj.nop())
                else:
                    inst = op.emit(eng_obj)
                if op.dma:
                    inst.then_inc(sems["d" + ename], 16)
                elif op.needs_sig:
                    inst.then_inc(sems["e" + ename], 1)
            return nwait

        stats = {}
        with nc.Block() as block:
            @block.tensor
            def _(e):
                stats["pe"] = replay(e, "pe")

            @block.scalar
            def _(e):
                stats["act"] = replay(e, "act")

            @block.vector
            def _(e):
                stats["dve"] = replay(e, "dve")

            @block.gpsimd
            def _(e):
                stats["pool"] = replay(e, "pool")

            @block.sync
            def _(e):
                stats["sp"] = replay(e, "sp")
        return {e: (len(per_eng[e]), stats[e]) for e in self.ENGINES}


class Arena:
    def __init__(self, nc, name, nbytes):
        self.t = nc.alloc_sbuf_tensor(name, [128, nbytes], mybir.dt.uint8)
        self.nbytes = nbytes
        self.off = 0

    def take(self, cols, dtype, align=64):
        esz = 2 if dtype == BF16 else 4
        self.off = (self.off + align - 1) // align * align
        start = self.off
        self.off += cols * esz
        assert self.off <= self.nbytes, f"arena overflow {self.off}>{self.nbytes}"
        return self.t[:, start:start + cols * esz].bitcast(dtype)

    def mark(self):
        return self.off

    def reset(self, off=0):
        self.off = off


def dap(handle, offset, dims):
    return bass.AP(handle, offset, [list(d) for d in dims])


def build_program(dbg=(), phases=None, races=True):
    nc = bass.Bass("TRN2", target_bir_lowering=False, detect_race_conditions=races)
    P = Plan()

    def din(name, shape):
        return nc.dram_tensor(name, list(shape), F32, kind="ExternalInput")

    x_d = din("x", (T, D))
    rms1_d = din("rms1_w", (D,))
    ipw_d = din("in_proj_w", (ROWS_IP, D))
    dtw_d = din("dt_proj_w", (DI, DTR))
    dtb_d = din("dt_proj_b", (DI,))
    alog_d = din("A_log", (DI, DST))
    dsk_d = din("D_skip", (DI,))
    opw_d = din("out_proj_w", (D, DI))
    rms2_d = din("rms2_w", (D,))
    gw_d = din("gate_w", (INTER, D))
    uw_d = din("up_w", (INTER, D))
    dw_d = din("down_w", (D, INTER))
    wr_d = din("Wr", (D, E))
    wg_d = din("Wg", (E, D, FE))
    wu_d = din("Wu", (E, D, FE))
    wd_d = din("Wd", (E, FE, D))
    sg_d = din("Sg", (D, FS))
    su_d = din("Su", (D, FS))
    sd_d = din("Sd", (FS, D))
    wsg_d = din("Wsg", (D, 1))
    y_d = nc.dram_tensor("y", [T, D], F32, kind="ExternalOutput")

    def scratch(name, shape, dtype):
        kind = "ExternalOutput" if name in dbg else "Internal"
        return nc.dram_tensor(name, list(shape), dtype, kind=kind)

    s_ipT = scratch("s_ipT", (NOC, 128, 2048), BF16)
    s_dtT = scratch("s_dtT", (128, 32 * 128), BF16)
    s_opT = scratch("s_opT", (32, 128, 2048), BF16)
    s_gT = scratch("s_gT", (64, 128, 2048), BF16)
    s_uT = scratch("s_uT", (64, 128, 2048), BF16)
    s_dnT = scratch("s_dnT", (64, 128, 2048), BF16)
    s_wg = scratch("s_wg", (E, 11, 128, 2048), BF16)
    s_wu = scratch("s_wu", (E, 11, 128, 2048), BF16)
    s_wd = scratch("s_wd", (E, FE, D), BF16)
    s_sg = scratch("s_sg", (44, 128, 2048), BF16)
    s_su = scratch("s_su", (44, 128, 2048), BF16)
    s_sd = scratch("s_sd", (FS, D), BF16)
    sp_dt = scratch("sp_dt", (16, GC, T), BF16)
    sp_dtx = scratch("sp_dtx", (16, GC, T), BF16)
    sp_xe = scratch("sp_xe", (16, GC, T), BF16)
    sp_B = scratch("sp_B", (16, GC, T), BF16)
    sp_C = scratch("sp_C", (16, GC, T), BF16)
    sp_sz = scratch("sp_sz", (16, GC, T), BF16)
    sp_y2 = scratch("sp_y2", (DI, T), BF16)
    sp_s = scratch("sp_s", (T, D), F32)
    sp_hn = scratch("sp_hn", (16, 128, T), BF16)
    sp_x1 = scratch("sp_x1", (T, D), F32)
    sp_x1f = scratch("sp_x1f", (16, 128, T), BF16)
    dbg_act = scratch("dbg_act", (8, 128, TQ), BF16)
    dbg_mlp = scratch("dbg_mlp", (4, 128, D), F32)
    dbg_sm = scratch("dbg_sm", (8, 128, 16), F32)
    dbg_C = scratch("dbg_C", (4, 128, 8), F32)
    dbg_moe = scratch("dbg_moe", (4, 128, 2048), F32)
    dbg_sh = scratch("dbg_sh", (4, 128, 2048), F32)
    dbg_sig = scratch("dbg_sig", (128, 4), F32)
    dbg_rps = scratch("dbg_rps", (4, 128, 40), F32)
    dbg_ctr = {"i": 0}

    eye_d = nc.inline_tensor(np.eye(128, dtype=np.float32), "eye_f32")
    selh_np = np.zeros((4, 128, 128), dtype=np.float32)
    for par in range(2):
        for h in range(2):
            for j in range(128):
                selh_np[par * 2 + h, 64 * par + 32 * h + j // 4, j] = 1.0
    sel_d = nc.inline_tensor(selh_np, "sel_f32")
    ones_d = nc.inline_tensor(np.ones((1, 128), dtype=np.float32), "ones_f32")

    STAT = Arena(nc, "stat", 79 * 1024)
    eye_bf = STAT.take(128, BF16)
    sel_bf = [STAT.take(128, BF16) for _ in range(4)]
    ones_bf = STAT.take(128, BF16)
    rms1_bc = STAT.take(D, BF16)
    rms2_bc = STAT.take(D, BF16)
    aneg = [STAT.take(256, F32) for _ in range(2)]
    dsk_sb = [STAT.take(16, F32) for _ in range(2)]
    dtb_sb = [STAT.take(16, F32) for _ in range(2)]
    dtT_sb = STAT.take(32 * 128, BF16)
    wr_sb = STAT.take(16 * E, BF16)
    wsg_sb = STAT.take(16, BF16)
    wring = [STAT.take(2048, BF16) for _ in range(4)]
    stg = [STAT.take(2048, BF16) for _ in range(4)]
    stg32 = [STAT.take(2048, F32) for _ in range(2)]
    small = [STAT.take(16, F32) for _ in range(8)]
    cst = STAT.take(4, F32)
    ln0 = STAT.take(D, F32)

    ARN = Arena(nc, "arena", 125 * 1024)
    m0 = ARN.mark()
    hf_q = [ARN.take(TQ, BF16) for _ in range(16)]
    xq = [ARN.take(D, F32) for _ in range(2)]
    xv_sb = [ARN.take(TQ, BF16) for _ in range(8)]
    bm_sb = [ARN.take(TQ, BF16) for _ in range(8)]
    dtin_sb = ARN.take(TQ, BF16)
    sptmp = [ARN.take(TQ, F32) for _ in range(2)]
    r1_end = ARN.mark()
    bstr = [ARN.take(4 * TQ, BF16) for _ in range(2)]
    cstr = [ARN.take(4 * TQ, BF16) for _ in range(2)]
    dtc_sb = ARN.take(2 * TQ, BF16)
    dtxc_sb = ARN.take(2 * TQ, BF16)
    xec_sb = ARN.take(2 * TQ, BF16)
    szc_sb = ARN.take(2 * TQ, BF16)
    da_sb = [ARN.take(TQ, F32) for _ in range(2)]
    dbx_sb = [ARN.take(TQ, BF16) for _ in range(2)]
    hring = ARN.take(16 * TQ, BF16)
    yc_sb = ARN.take(TQ, F32)
    ytmp = ARN.take(TQ, F32)
    y2st = [ARN.take(TQ, BF16) for _ in range(2)]
    carry = [ARN.take(256, F32) for _ in range(2)]
    r2_end = ARN.mark()
    y2ring = [ARN.take(TQ, BF16) for _ in range(3)]
    h1t = ARN.take(D, F32)
    r3_end = ARN.mark()

    R1_KEYS = ([("hf", dc, tc) for dc in range(16) for tc in range(4)]
               + [("xq", i) for i in range(2)]
               + [("xv", a) for a in range(8)] + [("bm", a) for a in range(8)]
               + [("dtin",)] + [("sptmp", i) for i in range(2)])
    R2_KEYS = ([("bstr", i) for i in range(2)] + [("cstr", i) for i in range(2)]
               + [("dtc",), ("dtxc",), ("xec",), ("szc",)]
               + [("da", i) for i in range(2)] + [("dbx", i) for i in range(2)]
               + [("hr", n) for n in range(16)] + [("yc",), ("ytmp",)]
               + [("y2st", i) for i in range(2)] + [("carry", h) for h in range(2)])
    R3_KEYS = [("y2ring", i) for i in range(3)] + [("h1t",)]

    ARN.reset(m0)
    hnf_q = [ARN.take(TQ, BF16) for _ in range(16)]
    mlp_acc = [ARN.take(D, F32) for _ in range(4)]
    actt = [ARN.take(TQ, BF16) for _ in range(4)]
    p3_end = ARN.mark()
    assert p3_end <= r2_end, (p3_end, r2_end)
    P3_KEYS = ([("hnf", dc) for dc in range(16)]
               + [("mlp", tc, o) for tc in range(4) for o in range(4)]
               + [("actt", j) for j in range(4)])

    ARN.reset(m0)
    x1f_q = [ARN.take(TQ, BF16) for _ in range(16)]
    acte = [ARN.take(TQ, BF16) for _ in range(11)]
    wde = [ARN.take(D, BF16) for _ in range(11)]
    moe_acc = [ARN.take(D, F32) for _ in range(4)]
    ctile = [ARN.take(16, F32) for _ in range(4)]
    sigt = ARN.take(16, F32)
    rps = ARN.take(64, F32)
    p4_end = ARN.mark()
    moe_start = p4_end - (4 * D * 4 + 4 * 16 * 4 + 16 * 4 + 64 * 4 + 512)
    ARN.reset(m0 + 16 * TQ * 2)
    sh_acc = [ARN.take(D, F32) for _ in range(4)]
    actsh = [ARN.take(TQ, BF16) for _ in range(4)]
    p4b_end = ARN.mark()
    assert p4b_end <= moe_start, (p4b_end, moe_start, p4_end)
    P4A_KEYS = ([("x1f", dc) for dc in range(16)]
                + [("acte", j) for j in range(11)] + [("wde", j) for j in range(11)]
                + [("moe", tc) for tc in range(4)] + [("C", tc) for tc in range(4)]
                + [("sig",), ("rps",)])
    P4B_KEYS = ([("sh", tc, o) for tc in range(4) for o in range(4)]
                + [("actsh", j) for j in range(4)])

    psm = [nc.alloc_psum_tensor(f"psm{i}", [128, 512], F32) for i in range(4)]
    psx = [nc.alloc_psum_tensor(f"psx{i}", [128, 512], F32) for i in range(4)]
    print(f"[build] sbuf left/part={nc.sbuf_bytes_remaining} "
          f"arena R1={r1_end-m0} R2={r2_end-r1_end} R3={r3_end-r2_end} "
          f"p3={p3_end-m0} p4={p4_end-m0}")

    mainb = {"i": 0}
    auxb = {"i": 0}

    def next_main():
        mainb["i"] = (mainb["i"] + 1) % 4
        return mainb["i"]

    def next_aux():
        auxb["i"] = (auxb["i"] + 1) % 4
        return auxb["i"]

    def kpm(i):
        return ("psm", i)

    def kpx(i):
        return ("psx", i)

    def LD(dst_ap, src_ap, writes, reads=(), q="sp", tag="ld", slow=False):
        P.add(q, (lambda d, s, sl: (lambda e: e.dma_start(
            d, s, allow_slow_non_contiguous=sl)))(dst_ap, src_ap, slow),
              reads=tuple(reads), writes=tuple(writes), dma=True, tag=tag)

    def ST(dst_ap, src_ap, writes, reads, q="act", tag="st"):
        P.add(q, (lambda d, s: (lambda e: e.dma_start(d, s)))(dst_ap, src_ap),
              reads=tuple(reads), writes=tuple(writes), dma=True, tag=tag)

    P.add("dve", lambda e: e.memset(cst[:, 0:1], EPS), writes=[("cst",)])
    P.add("dve", lambda e: e.memset(cst[:, 1:2], 1.0 / D), reads=[("cst",)],
          writes=[("cst",)])
    LD(stg32[0][:, 0:128], eye_d.ap(), [("stg32", 0)])
    P.add("dve", lambda e: e.tensor_copy(eye_bf[:, 0:128], stg32[0][:, 0:128]),
          reads=[("stg32", 0)], writes=[("eye",)])
    for si in range(4):
        LD(stg32[0][:, 128:256], sel_d.ap()[si], [("stg32", 0)],
           reads=[("eye",)])
        P.add("dve", (lambda ss: lambda e: e.tensor_copy(
            sel_bf[ss][:, 0:128], stg32[0][:, 128:256]))(si),
            reads=[("stg32", 0)], writes=[("sel", si)])
    LD(stg32[1][0:1, 0:128], ones_d.ap(), [("stg32", 1)])
    P.add("dve", lambda e: e.tensor_copy(ones_bf[0:1, 0:128], stg32[1][0:1, 0:128]),
          reads=[("stg32", 1)], writes=[("ones",)])

    def row_bcast(vec_d, dst_bf, kout):
        LD(stg32[0][0:1, 0:D], dap(vec_d, 0, [[D, 1], [1, D]]), [("stg32", 0)])
        P.add("dve", lambda e: e.tensor_copy(stg[3][0:1, 0:D], stg32[0][0:1, 0:D]),
              reads=[("stg32", 0)], writes=[("stg", 3)])
        for j in range(4):
            bi = next_aux()
            P.add("pe", (lambda jj, b: lambda e: e.matmul(
                psx[b][:, 0:512], ones_bf[0:1, 0:128],
                stg[3][0:1, jj * 512:(jj + 1) * 512], start=True, stop=True))(j, bi),
                reads=[("stg", 3), ("ones",)], writes=[kpx(bi)])
            P.add("act", (lambda jj, b: lambda e: e.copy(
                dst_bf[:, jj * 512:(jj + 1) * 512], psx[b][:, 0:512]))(j, bi),
                reads=[kpx(bi)], writes=[kout])

    row_bcast(rms1_d, rms1_bc, ("rms1bc",))
    row_bcast(rms2_d, rms2_bc, ("rms2bc",))

    for h in range(2):
        LD(stg32[0][:, 0:256],
           dap(alog_d, (128 * h) * 256, [[256, 128], [1, 256]]),
           [("stg32", 0)])
        P.add("act", (lambda: lambda e: e.activation(
            stg32[1][:, 0:256], stg32[0][:, 0:256], AF.Exp))(),
            reads=[("stg32", 0)], writes=[("stg32", 1)])
        P.add("dve", (lambda hh: lambda e: e.tensor_scalar(
            aneg[hh][:, 0:256], stg32[1][:, 0:256], -1.0, None, ALU.mult))(h),
            reads=[("stg32", 1)], writes=[("aneg", h)])
        LD(dsk_sb[h][:, 0:16], dap(dsk_d, (128 * h) * 16, [[16, 128], [1, 16]]),
           [("dsk", h)])
        LD(dtb_sb[h][:, 0:16], dap(dtb_d, (128 * h) * 16, [[16, 128], [1, 16]]),
           [("dtb", h)])
    for dc in range(16):
        LD(stg32[0][:, dc * E:(dc + 1) * E],
           dap(wr_d, dc * 128 * E, [[E, 128], [1, E]]), [("stg32", 0)])
    P.add("dve", lambda e: e.tensor_copy(wr_sb[:, 0:16 * E], stg32[0][:, 0:16 * E]),
          reads=[("stg32", 0)], writes=[("wr",)])
    LD(stg32[0][:, 128:144], dap(wsg_d, 0, [[1, 128], [128, 16]]),
       [("stg32", 0)], slow=True)
    P.add("dve", lambda e: e.tensor_copy(wsg_sb[:, 0:16], stg32[0][:, 128:144]),
          reads=[("stg32", 0)], writes=[("wsg",)])

    tpc = {"i": 0}

    def transpose_chunk(src_ap, nblk, out_ap, out_key, tag, blk_out=None):
        i = tpc["i"]
        tpc["i"] += 1
        s32 = stg32[i % 2]
        w = wring[i % 4]
        st_ = stg[i % 4]
        if not isinstance(src_ap, list):
            src_ap = [((0, 128), src_ap)]
        for (p0, p1), sap in src_ap:
            LD(s32[p0:p1, 0:nblk * 128], sap, [("stg32", i % 2)], tag=tag + "ld")
        P.add("dve", (lambda n, ww, ss: lambda e: e.tensor_copy(
            ww[:, 0:n * 128], ss[:, 0:n * 128]))(nblk, w, s32),
            reads=[("stg32", i % 2)], writes=[("wring", i % 4)], tag=tag + "c")
        for b in range(nblk):
            bi = next_aux()
            P.add("pe", (lambda bb, bpi, ww: lambda e: e.matmul(
                psx[bpi][:, 0:128], ww[:, bb * 128:(bb + 1) * 128],
                eye_bf[:, 0:128], start=True, stop=True))(b, bi, w),
                reads=[("wring", i % 4), ("eye",)], writes=[kpx(bi)], tag=tag + "T")
            eng = "act" if b % 2 == 0 else "dve"
            P.add(eng, (lambda bb, bpi, ss, ee=eng: lambda e: (
                e.copy(ss[:, bb * 128:(bb + 1) * 128], psx[bpi][:, 0:128])
                if ee == "act" else
                e.tensor_copy(ss[:, bb * 128:(bb + 1) * 128], psx[bpi][:, 0:128])))(
                    b, bi, st_),
                reads=[kpx(bi)], writes=[("stg", i % 4)], tag=tag + "e")
            if blk_out is not None:
                dst_ap_b, key_b = blk_out[b]
                ST(dst_ap_b, st_[:, b * 128:(b + 1) * 128],
                   [key_b, ("stg", i % 4)], [("stg", i % 4)],
                   q="pool" if b % 2 else "act", tag=tag + "bst")
        if blk_out is None:
            ST(out_ap, st_[:, 0:nblk * 128], [out_key, ("stg", i % 4)],
               [("stg", i % 4)], q="pool", tag=tag + "st")

    def ip_rows_ap(kind, a, h):
        if kind == "z":
            return [((0, 128), dap(ipw_d, (2048 * h + a) * D,
                                   [[16 * D, 128], [1, D]]))]
        if kind in ("xv", "bm"):
            base = DI if kind == "xv" else DI + DXB
            return [((64 * jh, 64 * jh + 64),
                     dap(ipw_d, (base + 2 * a + jh) * D, [[16 * D, 64], [1, D]]))
                    for jh in range(2)]
        if kind == "cm":
            return [((0, 128), dap(ipw_d, (DI + 2 * DXB + 2048 * h + a) * D,
                                   [[16 * D, 128], [1, D]]))]
        return [((0, 128), dap(ipw_d, (2 * DI + 2 * DXB) * D, [[D, 128], [1, D]]))]

    def p0a():
        for oc, (kind, a, h) in enumerate(OC_LIST):
            transpose_chunk(
                ip_rows_ap(kind, a, h), 16,
                dap(s_ipT, oc * 128 * 2048, [[2048, 128], [1, 2048]]),
                ("ipT", oc), "ipT")
        for c in range(16):
            for h in range(2):
                k = c * 2 + h
                transpose_chunk(
                    dap(dtw_d, (2048 * h + c) * DTR, [[16 * DTR, 128], [1, DTR]]),
                    1, dap(s_dtT, k * 128, [[32 * 128, 128], [1, 128]]),
                    ("dtT", k), "dtT")
        LD(dtT_sb[:, 0:32 * 128], dap(s_dtT, 0, [[32 * 128, 128], [1, 32 * 128]]),
           [("dtTr",)], reads=[("dtT", k) for k in range(32)])

    def p0b():
        for ochunk in range(16):
            for half in range(2):
                blk = []
                for bb in range(16):
                    b = half * 16 + bb
                    blk.append((dap(s_opT, b * 128 * 2048 + ochunk * 128,
                                    [[2048, 128], [1, 128]]),
                                ("opTt", ochunk, b)))
                transpose_chunk(
                    dap(opw_d, ochunk * 128 * DI + half * 2048,
                        [[DI, 128], [1, 2048]]),
                    16, None, None, "opT", blk_out=blk)
        P.fence("sp", [("opTt", o, b) for o in range(16) for b in range(32)]
                + [("opT",)], tag="opfence")

    def p0c():
        for (wd_, s_out, tag) in ((gw_d, s_gT, "gT"), (uw_d, s_uT, "uT")):
            for ch in range(64):
                transpose_chunk(
                    dap(wd_, ch * 128 * D, [[D, 128], [1, D]]), 16,
                    dap(s_out, ch * 128 * 2048, [[2048, 128], [1, 2048]]),
                    (tag, ch), tag)
        for ch in range(16):
            for bg in range(4):
                blk = []
                for b in range(16):
                    fc = bg * 16 + b
                    blk.append((dap(s_dnT, fc * 128 * 2048 + ch * 128,
                                    [[2048, 128], [1, 128]]),
                                ("dnTc", ch, bg, b)))
                transpose_chunk(
                    dap(dw_d, ch * 128 * INTER + bg * 2048,
                        [[INTER, 128], [1, 2048]]),
                    16, None, None, "dnT", blk_out=blk)
        P.fence("sp", [("gT", c) for c in range(64)] + [("uT", c) for c in range(64)]
                + [("dnTc", c, b, bb) for c in range(16) for b in range(4)
                   for bb in range(16)] + [("dnTf",)],
                tag="p0cfence")

    def p0d():
        eng_rr = ["dve", "act", "pool"]
        ctr = {"ei": 0}

        def cast_tile(src_t, src_off, rows, cols, dst_sts):
            ei = ctr["ei"]
            ctr["ei"] += 1
            i32 = ei % 2
            sslot = ei % 4
            eng = eng_rr[ei % 3]
            LD(stg32[i32][0:rows, 0:cols], src_off, [("stg32", i32)], tag="ffld")
            P.add(eng, (lambda ii, s2, r_, c_, ee=eng: lambda e: (
                e.copy(stg[s2][0:r_, 0:c_], stg32[ii][0:r_, 0:c_])
                if ee == "act" else
                e.tensor_copy(stg[s2][0:r_, 0:c_], stg32[ii][0:r_, 0:c_])))(
                    i32, sslot, rows, cols),
                reads=[("stg32", i32)], writes=[("stg", sslot)], tag="ffc")
            for j, (dst_ap, c0, c1, key) in enumerate(dst_sts):
                ST(dst_ap, stg[sslot][0:rows, c0:c1], [key, ("stg", sslot)],
                   [("stg", sslot)], q="pool" if (ei + j) % 2 else "act",
                   tag="ffst")

        for ee in range(E):
            for (wsrc, wdst, nm) in ((wg_d, s_wg, "wg"), (wu_d, s_wu, "wu")):
                for rc in range(16):
                    sts = []
                    for fc in range(11):
                        dst = dap(wdst, (ee * 11 + fc) * 128 * 2048 + rc * 128,
                                  [[2048, 128], [1, 128]])
                        sts.append((dst, fc * 128, fc * 128 + 128,
                                    ("ffc", nm, ee, rc, fc)))
                    cast_tile(None, dap(wsrc, ee * D * FE + rc * 128 * FE,
                                        [[FE, 128], [1, FE]]), 128, FE, sts)
            for rc in range(11):
                dst = dap(s_wd, ee * FE * D + rc * 128 * D, [[D, 128], [1, D]])
                cast_tile(None, dap(wd_d, ee * FE * D + rc * 128 * D,
                                    [[D, 128], [1, D]]), 128, D,
                          [(dst, 0, D, ("ffc", "wd", ee, rc))])
        for (wsrc, wdst, nm) in ((sg_d, s_sg, "sg"), (su_d, s_su, "su")):
            for rc in range(16):
                for cp in range(3):
                    f0 = cp * 16
                    nfc = min(16, 44 - f0)
                    sts = []
                    for j in range(nfc):
                        fc = f0 + j
                        dst = dap(wdst, fc * 128 * 2048 + rc * 128,
                                  [[2048, 128], [1, 128]])
                        sts.append((dst, j * 128, j * 128 + 128,
                                    ("ffc", nm, rc, fc)))
                    cast_tile(None, dap(wsrc, rc * 128 * FS + f0 * 128,
                                        [[FS, 128], [1, nfc * 128]]),
                              128, nfc * 128, sts)
        for rc in range(44):
            dst = dap(s_sd, rc * 128 * D, [[D, 128], [1, D]])
            cast_tile(None, dap(sd_d, rc * 128 * D, [[D, 128], [1, D]]), 128, D,
                      [(dst, 0, D, ("ffc", "sd", rc))])
        P.fence("sp", [k for k in list(P.writer.keys())
                       if isinstance(k, tuple) and k and k[0] == "ffc"]
                + [("ffn_w",)], tag="fffence")

    def p1a(q):
        qoff = q * TQ
        for tc in range(4):
            xs = tc % 2
            LD(xq[xs][:, 0:D], dap(x_d, (qoff + tc * 128) * D, [[D, 128], [1, D]]),
               [("xq", xs)], tag="xld")
            P.add("act", (lambda s: lambda e: e.activation(
                stg32[1][:, 0:D], xq[s][:, 0:D], AF.Square))(xs),
                reads=[("xq", xs)], writes=[("stg32", 1)], tag="rsq")
            P.add("dve", lambda e: e.reduce_sum(small[0][:, 0:1], stg32[1][:, 0:D],
                                                axis=AX.X),
                  reads=[("stg32", 1)], writes=[("sm", 0)], tag="rsqr")
            P.add("act", lambda e: e.activation(
                small[1][:, 0:1], small[0][:, 0:1], AF.Sqrt, bias=cst[:, 0:1],
                scale=cst[:, 1:2]),
                reads=[("sm", 0), ("cst",)], writes=[("sm", 1)])
            P.add("dve", lambda e: e.reciprocal(small[2][:, 0:1], small[1][:, 0:1]),
                  reads=[("sm", 1)], writes=[("sm", 2)])
            hs = 2 + tc % 2
            P.add("dve", (lambda s, s2: lambda e: e.scalar_tensor_tensor(
                stg[s2][:, 0:D], xq[s][:, 0:D], small[2][:, 0:1],
                rms1_bc[:, 0:D], ALU.mult, ALU.mult))(xs, hs),
                reads=[("xq", xs), ("sm", 2), ("rms1bc",)], writes=[("stg", hs)],
                tag="rapp")
            for dc in range(16):
                bi = next_aux()
                P.add("pe", (lambda d_, b, s2: lambda e: e.matmul(
                    psx[b][:, 0:128], stg[s2][:, d_ * 128:(d_ + 1) * 128],
                    eye_bf[:, 0:128], start=True, stop=True))(dc, bi, hs),
                    reads=[("stg", hs), ("eye",)], writes=[kpx(bi)], tag="hT")
                P.add("act" if dc % 2 else "dve",
                      (lambda t_, d_, b, ee=("act" if dc % 2 else "dve"):
                       lambda e: (e.copy if ee == "act" else e.tensor_copy)(
                          hf_q[d_][:, t_ * 128:(t_ + 1) * 128], psx[b][:, 0:128]))(
                          tc, dc, bi),
                      reads=[kpx(bi)], writes=[("hf", dc, tc)], tag="hTe")
        def ipw_ld(oc2):
            LD(wring[oc2 % 4][:, 0:2048],
               dap(s_ipT, oc2 * 128 * 2048, [[2048, 128], [1, 2048]]),
               [("wring", oc2 % 4)], reads=[("ipT", oc2)], tag="ipw")
        ipw_ld(0)
        ipw_ld(1)
        for oc, (kind, a, h) in enumerate(OC_LIST):
            ws = oc % 4
            if oc + 2 < NOC:
                ipw_ld(oc + 2)
            lag = [("wring", (oc + 2) % 4)] if oc + 2 < NOC else (
                [("wring", (oc + 1) % 4)] if oc + 1 < NOC else [])
            bi = next_main()
            for dc in range(16):
                P.add("pe", (lambda w_, d_, b: lambda e: e.matmul(
                    psm[b][:, 0:TQ], wring[w_][:, d_ * 128:(d_ + 1) * 128],
                    hf_q[d_][:, 0:TQ], start=(d_ == 0), stop=(d_ == 15)))(ws, dc, bi),
                    reads=[("wring", ws)] + [("hf", dc, t2) for t2 in range(4)]
                    + lag, writes=[kpm(bi)], tag="ipmm")
            if kind == "z":
                c, hh = a, h
                ss = oc % 4
                P.add("act", (lambda b, s: lambda e: e.activation(
                    stg[s][:, 0:TQ], psm[b][:, 0:TQ], AF.Silu))(bi, ss),
                    reads=[kpm(bi)], writes=[("stg", ss)], tag="siluz")
                ST(dap(sp_sz, c * GC * T + hh * 128 * T + qoff, [[T, 128], [1, TQ]]),
                   stg[ss][:, 0:TQ], [("spsz", q, c, hh), ("stg", ss)],
                   [("stg", ss)])
            elif kind == "xv":
                P.add("act", (lambda b, aa: lambda e: e.copy(
                    xv_sb[aa][:, 0:TQ], psm[b][:, 0:TQ]))(bi, a),
                    reads=[kpm(bi)], writes=[("xv", a)], tag="xve")
            elif kind == "bm":
                P.add("act", (lambda b, aa: lambda e: e.copy(
                    bm_sb[aa][:, 0:TQ], psm[b][:, 0:TQ]))(bi, a),
                    reads=[kpm(bi)], writes=[("bm", a)], tag="bme")
            elif kind == "cm":
                n, hh = a, h
                ss = oc % 4
                P.add("act", (lambda b, s: lambda e: e.copy(
                    stg[s][:, 0:TQ], psm[b][:, 0:TQ]))(bi, ss),
                    reads=[kpm(bi)], writes=[("stg", ss)], tag="cme")
                ST(dap(sp_C, n * GC * T + hh * 128 * T + qoff, [[T, 128], [1, TQ]]),
                   stg[ss][:, 0:TQ], [("spC", q, n, hh), ("stg", ss)],
                   [("stg", ss)])
            else:
                P.add("act", (lambda b: lambda e: e.copy(
                    dtin_sb[:, 0:TQ], psm[b][:, 0:TQ]))(bi),
                    reads=[kpm(bi)], writes=[("dtin",)], tag="dtine")

        for c in range(16):
            for h in range(2):
                k = c * 2 + h
                bi = next_main()
                P.add("pe", (lambda kk, b: lambda e: e.matmul(
                    psm[b][:, 0:TQ], dtT_sb[:, kk * 128:(kk + 1) * 128],
                    dtin_sb[:, 0:TQ], start=True, stop=True))(k, bi),
                    reads=[("dtTr",), ("dtin",)], writes=[kpm(bi)], tag="dtmm")
                P.add("act", (lambda b, hh, cc: lambda e: e.activation(
                    sptmp[0][:, 0:TQ], psm[b][:, 0:TQ], AF.Relu,
                    bias=dtb_sb[hh][:, cc:cc + 1]))(bi, h, c),
                    reads=[kpm(bi), ("dtb", h)], writes=[("sptmp", 0)], tag="spr")
                P.add("act", (lambda b, hh, cc: lambda e: e.activation(
                    sptmp[1][:, 0:TQ], psm[b][:, 0:TQ], AF.Abs,
                    bias=dtb_sb[hh][:, cc:cc + 1]))(bi, h, c),
                    reads=[kpm(bi), ("dtb", h)], writes=[("sptmp", 1)], tag="spb")
                P.add("act", lambda e: e.activation(
                    sptmp[1][:, 0:TQ], sptmp[1][:, 0:TQ], AF.Exp, scale=-1.0),
                    reads=[("sptmp", 1)], writes=[("sptmp", 1)], tag="spe")
                P.add("act", lambda e: e.activation(
                    sptmp[1][:, 0:TQ], sptmp[1][:, 0:TQ], AF.Ln, bias=1.0),
                    reads=[("sptmp", 1)], writes=[("sptmp", 1)], tag="spl")
                ss = k % 4
                P.add("dve", (lambda s: lambda e: e.tensor_add(
                    stg[s][:, 0:TQ], sptmp[0][:, 0:TQ], sptmp[1][:, 0:TQ]))(ss),
                    reads=[("sptmp", 0), ("sptmp", 1)], writes=[("stg", ss)],
                    tag="spadd")
                ST(dap(sp_dt, c * GC * T + h * 128 * T + qoff, [[T, 128], [1, TQ]]),
                   stg[ss][:, 0:TQ], [("spdt", q, c, h), ("stg", ss)],
                   [("stg", ss)])
                bx = next_aux()
                si = (c % 2) * 2 + h
                P.add("pe", (lambda cc, b, ss: lambda e: e.matmul(
                    psx[b][:, 0:TQ], sel_bf[ss][:, 0:128],
                    xv_sb[cc // 2][:, 0:TQ],
                    start=True, stop=True))(c, bx, si),
                    reads=[("xv", c // 2), ("sel", si)], writes=[kpx(bx)], tag="xbc")
                s2 = (k + 1) % 4
                P.add("act", (lambda b, s: lambda e: e.copy(
                    stg[s][:, 0:TQ], psx[b][:, 0:TQ]))(bx, s2),
                    reads=[kpx(bx)], writes=[("stg", s2)], tag="xee")
                ST(dap(sp_xe, c * GC * T + h * 128 * T + qoff, [[T, 128], [1, TQ]]),
                   stg[s2][:, 0:TQ], [("spxe", q, c, h), ("stg", s2)],
                   [("stg", s2)])
                s3 = (k + 2) % 4
                P.add("pool", (lambda sA, sB2, sC: lambda e: e.tensor_mul(
                    stg[sC][:, 0:TQ], stg[sA][:, 0:TQ], stg[sB2][:, 0:TQ]))(ss, s2, s3),
                    reads=[("stg", ss), ("stg", s2)], writes=[("stg", s3)], tag="dtx")
                ST(dap(sp_dtx, c * GC * T + h * 128 * T + qoff, [[T, 128], [1, TQ]]),
                   stg[s3][:, 0:TQ], [("spdtx", q, c, h), ("stg", s3)],
                   [("stg", s3)], q="pool")
        for n in range(16):
            for h in range(2):
                bx = next_aux()
                si = (n % 2) * 2 + h
                P.add("pe", (lambda nn, b, ss: lambda e: e.matmul(
                    psx[b][:, 0:TQ], sel_bf[ss][:, 0:128],
                    bm_sb[nn // 2][:, 0:TQ],
                    start=True, stop=True))(n, bx, si),
                    reads=[("bm", n // 2), ("sel", si)], writes=[kpx(bx)], tag="bbc")
                s2 = (n * 2 + h) % 4
                P.add("act", (lambda b, s: lambda e: e.copy(
                    stg[s][:, 0:TQ], psx[b][:, 0:TQ]))(bx, s2),
                    reads=[kpx(bx)], writes=[("stg", s2)], tag="bee")
                ST(dap(sp_B, n * GC * T + h * 128 * T + qoff, [[T, 128], [1, TQ]]),
                   stg[s2][:, 0:TQ], [("spB", q, n, h), ("stg", s2)],
                   [("stg", s2)])

    def p1b(q):
        qoff = q * TQ
        for c in range(16):
            for (kname, sb, sp, wk) in (("dtc", dtc_sb, sp_dt, "spdt"),
                                        ("dtxc", dtxc_sb, sp_dtx, "spdtx"),
                                        ("xec", xec_sb, sp_xe, "spxe"),
                                        ("szc", szc_sb, sp_sz, "spsz")):
                for hh in range(2):
                    LD(sb[:, hh * TQ:(hh + 1) * TQ],
                       dap(sp, c * GC * T + hh * 128 * T + qoff,
                           [[T, 128], [1, TQ]]),
                       [(kname,)], reads=[(wk, q, c, hh)], q="pool",
                       tag=kname + "ld")
            for h in range(2):
                for ng in range(4):
                    g2 = ng % 2
                    for nin in range(4):
                        nn = ng * 4 + nin
                        LD(bstr[g2][:, nin * TQ:(nin + 1) * TQ],
                           dap(sp_B, nn * GC * T + h * 128 * T + qoff,
                               [[T, 128], [1, TQ]]),
                           [("bstr", g2)], reads=[("spB", q, nn, h)], tag="bsld")
                        LD(cstr[g2][:, nin * TQ:(nin + 1) * TQ],
                           dap(sp_C, nn * GC * T + h * 128 * T + qoff,
                               [[T, 128], [1, TQ]]),
                           [("cstr", g2)], reads=[("spC", q, nn, h)], q="pool",
                           tag="csld")
                    for nin in range(4):
                        n = ng * 4 + nin
                        ds = n % 2
                        P.add("act", (lambda cc, hh, nn, s: lambda e: e.activation(
                            da_sb[s][:, 0:TQ], dtc_sb[:, hh * TQ:(hh + 1) * TQ],
                            AF.Exp,
                            scale=aneg[hh][:, cc * 16 + nn:cc * 16 + nn + 1]))(
                                c, h, n, ds),
                            reads=[("dtc",), ("aneg", h)], writes=[("da", ds)],
                            tag="dA")
                        P.add("pool", (lambda hh, ni, s, g: lambda e: e.tensor_mul(
                            dbx_sb[s][:, 0:TQ], dtxc_sb[:, hh * TQ:(hh + 1) * TQ],
                            bstr[g][:, ni * TQ:(ni + 1) * TQ]))(h, nin, ds, g2),
                            reads=[("dtxc",), ("bstr", g2)], writes=[("dbx", ds)],
                            tag="dBx")
                        if q == 0:
                            init = 0.0
                            rd_i = []
                        else:
                            init = carry[h][:, c * 16 + n:c * 16 + n + 1]
                            rd_i = [("carry", h)]
                        P.add("dve", (lambda s, nn, ini: lambda e:
                              e.tensor_tensor_scan(
                                  hring[:, nn * TQ:(nn + 1) * TQ],
                                  da_sb[s][:, 0:TQ], dbx_sb[s][:, 0:TQ], ini,
                                  ALU.mult, ALU.add))(ds, n, init),
                              reads=[("da", ds), ("dbx", ds)] + rd_i,
                              writes=[("hr", n)], tag="scan")
                    hv = hring[:, 0:16 * TQ].rearrange("p (n t) -> p n t", n=16)
                    P.add("act", (lambda cc, hh, gg, hvv: lambda e: e.copy(
                        carry[hh][:, cc * 16 + gg * 4:cc * 16 + gg * 4 + 4]
                        .unsqueeze(2),
                        hvv[:, gg * 4:gg * 4 + 4, TQ - 1:TQ]))(c, h, ng, hv),
                        reads=[("hr", nn2) for nn2 in range(ng * 4, ng * 4 + 4)],
                        writes=[("carry", h)], tag="carry")
                    for nin in range(4):
                        n = ng * 4 + nin
                        P.add("dve", (lambda nn, g, ni: lambda e: e.tensor_mul(
                            hring[:, nn * TQ:(nn + 1) * TQ],
                            hring[:, nn * TQ:(nn + 1) * TQ],
                            cstr[g][:, ni * TQ:(ni + 1) * TQ]))(n, g2, nin),
                            reads=[("hr", n), ("cstr", g2)], writes=[("hr", n)],
                            tag="prod")
                P.add("dve", (lambda: lambda e: e.reduce_sum(
                    yc_sb[:, 0:TQ].unsqueeze(2),
                    hring[:, 0:16 * TQ].rearrange("p (n t) -> p t n", n=16),
                    axis=AX.X))(),
                    reads=[("hr", nn2) for nn2 in range(16)], writes=[("yc",)],
                    tag="yred")
                P.add("dve", (lambda hh, cc: lambda e: e.scalar_tensor_tensor(
                    ytmp[:, 0:TQ], xec_sb[:, hh * TQ:(hh + 1) * TQ],
                    dsk_sb[hh][:, cc:cc + 1], yc_sb[:, 0:TQ], ALU.mult, ALU.add))(
                        h, c),
                    reads=[("yc",), ("xec",), ("dsk", h)], writes=[("ytmp",)],
                    tag="ydsk")
                ys = (c * 2 + h) % 2
                P.add("dve", (lambda hh, s: lambda e: e.tensor_mul(
                    y2st[s][:, 0:TQ], ytmp[:, 0:TQ],
                    szc_sb[:, hh * TQ:(hh + 1) * TQ]))(h, ys),
                    reads=[("ytmp",), ("szc",)], writes=[("y2st", ys)], tag="y2")
                ST(dap(sp_y2, (128 * h * 16 + c) * T + qoff, [[16 * T, 128], [1, TQ]]),
                   y2st[ys][:, 0:TQ], [("spy2", q, c * 2 + h), ("y2st", ys)],
                   [("y2st", ys)], q="pool")

    def p2(q):
        qoff = q * TQ
        h1s = [xq[0], xq[1], ln0, h1t]
        h1k = [("xq", 0), ("xq", 1), ("ln0",), ("h1t",)]
        def w8(i):
            return wring[i // 4][:, (i % 4) * 512:(i % 4) * 512 + 512]
        def w8k(i):
            return ("w8", i)
        def y7(i):
            return y2ring[i][:, 0:TQ]
        def y7k(i):
            return ("y2ring", i)
        for osub in range(4):
            def emit_ld(k2):
                LD(y7(k2 % 3), dap(sp_y2, k2 * 128 * T + qoff, [[T, 128], [1, TQ]]),
                   [y7k(k2 % 3)],
                   reads=[("spy2", q, c2 * 2 + (k2 // 16)) for c2 in range(16)],
                   q="pool", tag="y2ld")
                LD(w8(k2 % 8), dap(s_opT, k2 * 128 * 2048 + osub * 512,
                                   [[2048, 128], [1, 512]]),
                   [w8k(k2 % 8)], reads=[("opT",)], q="pool", tag="opwld")
            emit_ld(0)
            emit_ld(1)
            for kk in range(32):
                if kk + 2 < 32:
                    emit_ld(kk + 2)
                ws = kk % 8
                ys = kk % 3
                lag = [w8k((kk + 2) % 8), y7k((kk + 2) % 3)] if kk + 2 < 32 else                       ([w8k((kk + 1) % 8), y7k((kk + 1) % 3)] if kk + 1 < 32 else [])
                for tc in range(4):
                    P.add("pe", (lambda w_, y_, t_, k2: lambda e: e.matmul(
                        psm[t_][:, 0:512], y7(y_)[:, t_ * 128:(t_ + 1) * 128],
                        w8(w_), start=(k2 == 0), stop=(k2 == 31)))(ws, ys, tc, kk),
                        reads=[w8k(ws), y7k(ys)] + lag, writes=[kpm(tc)],
                        tag="opmm")
            for tc in range(4):
                LD(stg32[1][:, 0:512],
                   dap(x_d, (qoff + tc * 128) * D + osub * 512,
                       [[D, 128], [1, 512]]),
                   [("stg32", 1)], tag="p2x")
                P.add("dve", (lambda t_, o_: lambda e: e.tensor_add(
                    h1s[t_][:, o_ * 512:(o_ + 1) * 512], stg32[1][:, 0:512],
                    psm[t_][:, 0:512]))(tc, osub),
                    reads=[("stg32", 1), kpm(tc)], writes=[h1k[tc]], tag="h1ev")
                P.add("dve", (lambda t_, o_: lambda e: e.tensor_add(
                    stg32[0][:, 0:512], h1s[t_][:, o_ * 512:(o_ + 1) * 512],
                    stg32[1][:, 0:512]))(tc, osub),
                    reads=[("stg32", 1), h1k[tc]], writes=[("stg32", 0)], tag="sev")
                ST(dap(sp_s, (qoff + tc * 128) * D + osub * 512,
                       [[D, 128], [1, 512]]),
                   stg32[0][:, 0:512], [("sps", q, tc, osub), ("stg32", 0)],
                   [("stg32", 0)])
        for tc in range(4):
            P.add("act", (lambda t_: lambda e: e.activation(
                stg32[1][:, 0:D], h1s[t_][:, 0:D], AF.Square))(tc),
                reads=[h1k[tc]], writes=[("stg32", 1)], tag="r2sq")
            P.add("dve", lambda e: e.reduce_sum(small[0][:, 0:1], stg32[1][:, 0:D],
                                                axis=AX.X),
                  reads=[("stg32", 1)], writes=[("sm", 0)], tag="r2sqr")
            P.add("act", lambda e: e.activation(
                small[1][:, 0:1], small[0][:, 0:1], AF.Sqrt, bias=cst[:, 0:1],
                scale=cst[:, 1:2]),
                reads=[("sm", 0), ("cst",)], writes=[("sm", 1)])
            P.add("dve", lambda e: e.reciprocal(small[2][:, 0:1], small[1][:, 0:1]),
                  reads=[("sm", 1)], writes=[("sm", 2)])
            P.add("dve", (lambda t_: lambda e: e.scalar_tensor_tensor(
                stg[0][:, 0:D], h1s[t_][:, 0:D], small[2][:, 0:1], rms2_bc[:, 0:D],
                ALU.mult, ALU.mult))(tc),
                reads=[h1k[tc], ("sm", 2), ("rms2bc",)], writes=[("stg", 0)],
                tag="r2app")
            for dc in range(16):
                bi = next_aux()
                P.add("pe", (lambda d_, b: lambda e: e.matmul(
                    psx[b][:, 0:128], stg[0][:, d_ * 128:(d_ + 1) * 128],
                    eye_bf[:, 0:128], start=True, stop=True))(dc, bi),
                    reads=[("stg", 0), ("eye",)], writes=[kpx(bi)], tag="hnT")
                ss = 1 + dc % 3
                P.add("act" if dc % 2 else "dve",
                      (lambda b, s, ee=("act" if dc % 2 else "dve"): lambda e:
                       (e.copy if ee == "act" else e.tensor_copy)(
                          stg[s][:, 0:128], psx[b][:, 0:128]))(bi, ss),
                      reads=[kpx(bi)], writes=[("stg", ss)], tag="hnTe")
                ST(dap(sp_hn, dc * 128 * T + qoff + tc * 128, [[T, 128], [1, 128]]),
                   stg[ss][:, 0:128], [("sphn", q, dc, tc), ("stg", ss)],
                   [("stg", ss)], q="act")

    def ln_rows(kpre, out32_ap, kout, also_bf16_T=None):
        P.add("dve", lambda e: e.reduce_sum(small[3][:, 0:1], ln0[:, 0:D], axis=AX.X),
              reads=[kpre], writes=[("sm", 3)])
        P.add("act", lambda e: e.activation(
            stg32[1][:, 0:D], ln0[:, 0:D], AF.Square),
            reads=[kpre], writes=[("stg32", 1)])
        P.add("dve", lambda e: e.reduce_sum(small[4][:, 0:1], stg32[1][:, 0:D],
                                            axis=AX.X),
              reads=[("stg32", 1)], writes=[("sm", 4)])
        P.add("dve", lambda e: e.tensor_scalar(
            small[3][:, 0:1], small[3][:, 0:1], 1.0 / D, None, ALU.mult),
            reads=[("sm", 3)], writes=[("sm", 3)])
        P.add("dve", lambda e: e.tensor_mul(small[5][:, 0:1], small[3][:, 0:1],
                                            small[3][:, 0:1]),
              reads=[("sm", 3)], writes=[("sm", 5)])
        P.add("dve", lambda e: e.scalar_tensor_tensor(
            small[5][:, 0:1], small[4][:, 0:1], 1.0 / D, small[5][:, 0:1],
            ALU.mult, ALU.subtract),
            reads=[("sm", 4), ("sm", 5)], writes=[("sm", 5)])
        P.add("act", lambda e: e.activation(small[6][:, 0:1], small[5][:, 0:1],
                                            AF.Sqrt, bias=cst[:, 0:1]),
              reads=[("sm", 5), ("cst",)], writes=[("sm", 6)])
        P.add("dve", lambda e: e.reciprocal(small[6][:, 0:1], small[6][:, 0:1]),
              reads=[("sm", 6)], writes=[("sm", 6)])
        P.add("dve", lambda e: e.tensor_mul(small[7][:, 0:1], small[3][:, 0:1],
                                            small[6][:, 0:1]),
              reads=[("sm", 3), ("sm", 6)], writes=[("sm", 7)])
        P.add("dve", lambda e: e.tensor_scalar(small[7][:, 0:1], small[7][:, 0:1],
                                               -1.0, None, ALU.mult),
              reads=[("sm", 7)], writes=[("sm", 7)])
        if DBG_P3 and dbg_ctr["i"] < 8:
            ii = dbg_ctr["i"]
            dbg_ctr["i"] += 1
            for jj, slot in enumerate((3, 4, 5, 6, 7)):
                P.add("dve", (lambda s_, j_: lambda e: e.tensor_copy(
                    stg32[0][:, 64 + j_ * 2:64 + j_ * 2 + 1], small[s_][:, 0:1]))(
                        slot, jj),
                    reads=[("sm", slot)], writes=[("stg32", 0)], tag="dbgsm_c")
            ST(dap(dbg_sm, ii * 128 * 16, [[16, 128], [1, 16]]),
               stg32[0][:, 64:80], [("dbgsm", ii)], [("stg32", 0)], q="pool",
               tag="dbgsm")
        P.add("dve", lambda e: e.tensor_scalar(
            stg32[0][:, 0:D], ln0[:, 0:D], small[6][:, 0:1], small[7][:, 0:1],
            ALU.mult, ALU.add),
            reads=[kpre, ("sm", 6), ("sm", 7)], writes=[("stg32", 0)], tag="lnapp")
        ST(out32_ap, stg32[0][:, 0:D], [kout, ("stg32", 0)], [("stg32", 0)])
        if also_bf16_T is not None:
            dst_f_aps, keyf = also_bf16_T
            P.add("dve", lambda e: e.tensor_copy(stg[0][:, 0:D], stg32[0][:, 0:D]),
                  reads=[("stg32", 0)], writes=[("stg", 0)])
            for dc in range(16):
                bi = next_aux()
                P.add("pe", (lambda d_, b: lambda e: e.matmul(
                    psx[b][:, 0:128], stg[0][:, d_ * 128:(d_ + 1) * 128],
                    eye_bf[:, 0:128], start=True, stop=True))(dc, bi),
                    reads=[("stg", 0), ("eye",)], writes=[kpx(bi)], tag="x1T")
                ss = 1 + dc % 3
                P.add("act" if dc % 2 else "dve",
                      (lambda b, s, ee=("act" if dc % 2 else "dve"): lambda e:
                       (e.copy if ee == "act" else e.tensor_copy)(
                          stg[s][:, 0:128], psx[b][:, 0:128]))(bi, ss),
                      reads=[kpx(bi)], writes=[("stg", ss)], tag="x1Te")
                ST(dst_f_aps[dc], stg[ss][:, 0:128], [keyf(dc), ("stg", ss)],
                   [("stg", ss)], q="act")

    def p3(q):
        qoff = q * TQ
        for dc in range(16):
            LD(hnf_q[dc][:, 0:TQ],
               dap(sp_hn, dc * 128 * T + qoff, [[T, 128], [1, TQ]]),
               [("hnf", dc)], reads=[("sphn", q, dc, tc) for tc in range(4)],
               q="pool", tag="hnld")
        for fg in range(16):
            def gu_ld(j2):
                fc2 = fg * 4 + j2
                LD(wring[(2 * j2) % 4][:, 0:2048],
                   dap(s_gT, fc2 * 128 * 2048, [[2048, 128], [1, 2048]]),
                   [("wring", (2 * j2) % 4)], reads=[("gT", fc2)], tag="gld")
                LD(wring[(2 * j2 + 1) % 4][:, 0:2048],
                   dap(s_uT, fc2 * 128 * 2048, [[2048, 128], [1, 2048]]),
                   [("wring", (2 * j2 + 1) % 4)], reads=[("uT", fc2)], tag="uld")

            def dn_ld(j2):
                LD(wring[j2][:, 0:2048],
                   dap(s_dnT, (fg * 4 + j2) * 128 * 2048, [[2048, 128], [1, 2048]]),
                   [("wring", j2)], reads=[("dnTc", 0, 0, 0)], tag="dnld")
            gu_ld(0)
            for j in range(4):
                fc = fg * 4 + j
                wgs, wus = (2 * j) % 4, (2 * j + 1) % 4
                if j + 1 < 4:
                    gu_ld(j + 1)
                    lag = [("wring", (2 * j + 2) % 4), ("wring", (2 * j + 3) % 4)]
                else:
                    dn_ld(0)
                    dn_ld(1)
                    lag = [("wring", 0), ("wring", 1)]
                bg = next_main()
                bu = next_main()
                for dc in range(16):
                    P.add("pe", (lambda w_, d_, b: lambda e: e.matmul(
                        psm[b][:, 0:TQ], wring[w_][:, d_ * 128:(d_ + 1) * 128],
                        hnf_q[d_][:, 0:TQ], start=(d_ == 0), stop=(d_ == 15)))(
                            wgs, dc, bg),
                        reads=[("wring", wgs), ("hnf", dc), ("wring", wus)] + lag,
                        writes=[kpm(bg)], tag="gmm")
                for dc in range(16):
                    P.add("pe", (lambda w_, d_, b: lambda e: e.matmul(
                        psm[b][:, 0:TQ], wring[w_][:, d_ * 128:(d_ + 1) * 128],
                        hnf_q[d_][:, 0:TQ], start=(d_ == 0), stop=(d_ == 15)))(
                            wus, dc, bu),
                        reads=[("wring", wus), ("hnf", dc)] + lag, writes=[kpm(bu)],
                        tag="umm")
                P.add("act", (lambda b: lambda e: e.activation(
                    stg32[1][:, 0:TQ], psm[b][:, 0:TQ], AF.Silu))(bg),
                    reads=[kpm(bg)], writes=[("stg32", 1)], tag="gsilu")
                P.add("dve", (lambda b, jj: lambda e: e.tensor_mul(
                    actt[jj][:, 0:TQ], stg32[1][:, 0:TQ], psm[b][:, 0:TQ]))(bu, j),
                    reads=[("stg32", 1), kpm(bu)], writes=[("actt", j)], tag="gumul")
                if DBG_P3 and q == 0 and fg < 2:
                    ST(dap(dbg_act, (fg * 4 + j) * 128 * TQ, [[TQ, 128], [1, TQ]]),
                       actt[j][:, 0:TQ], [("dbga", fg, j)], [("actt", j)],
                       q="pool", tag="dbga")
            dn_ld(2)
            dn_ld(3)
            for tc in range(4):
                for osub in range(4):
                    bi = next_main()
                    for j in range(4):
                        P.add("pe", (lambda jj, t_, o_, b: lambda e: e.matmul(
                            psm[b][:, 0:512], actt[jj][:, t_ * 128:(t_ + 1) * 128],
                            wring[jj][:, o_ * 512:(o_ + 1) * 512],
                            start=(jj == 0), stop=(jj == 3)))(j, tc, osub, bi),
                            reads=[("actt", j), ("wring", j), ("wring", 3)],
                            writes=[kpm(bi)], tag="dnmm")
                    if fg == 0:
                        P.add("dve", (lambda t_, o_, b: lambda e: e.tensor_copy(
                            mlp_acc[t_][:, o_ * 512:(o_ + 1) * 512],
                            psm[b][:, 0:512]))(tc, osub, bi),
                            reads=[kpm(bi)], writes=[("mlp", tc, osub)], tag="dncp")
                    else:
                        P.add("dve", (lambda t_, o_, b: lambda e: e.tensor_add(
                            mlp_acc[t_][:, o_ * 512:(o_ + 1) * 512],
                            mlp_acc[t_][:, o_ * 512:(o_ + 1) * 512],
                            psm[b][:, 0:512]))(tc, osub, bi),
                            reads=[kpm(bi), ("mlp", tc, osub)],
                            writes=[("mlp", tc, osub)], tag="dnadd")
        for tc in range(4):
            if DBG_P3 and q == 0:
                ST(dap(dbg_mlp, tc * 128 * D, [[D, 128], [1, D]]),
                   mlp_acc[tc][:, 0:D],
                   [("dbgm", tc)], [("mlp", tc, o) for o in range(4)],
                   q="pool", tag="dbgm")
            LD(h1t[:, 0:D], dap(sp_s, (qoff + tc * 128) * D, [[D, 128], [1, D]]),
               [("h1t",)], reads=[("sps", q, tc, o) for o in range(4)], q="pool",
               tag="sld")
            LD(hnf_q[tc][:, 0:TQ],
               dap(sp_hn, tc * 128 * T + qoff, [[T, 128], [1, TQ]]),
               [("hnf", tc)], reads=[("sphn", q, tc, t2) for t2 in range(4)],
               q="pool", tag="slag")
            LD(hnf_q[4 + tc][:, 0:TQ],
               dap(sp_hn, (4 + tc) * 128 * T + qoff, [[T, 128], [1, TQ]]),
               [("hnf", 4 + tc)],
               reads=[("sphn", q, 4 + tc, t2) for t2 in range(4)],
               q="pool", tag="slag2")
            kpre = ("lnpre",)
            P.add("dve", (lambda t_: lambda e: e.tensor_add(
                ln0[:, 0:D], h1t[:, 0:D], mlp_acc[t_][:, 0:D]))(tc),
                reads=[("h1t",), ("hnf", tc), ("hnf", 4 + tc)]
                + [("mlp", tc, o) for o in range(4)],
                writes=[kpre, ("ln0",)], tag="x1sum")
            ln_rows(kpre,
                    dap(sp_x1, (qoff + tc * 128) * D, [[D, 128], [1, D]]),
                    ("spx1", q, tc),
                    ([dap(sp_x1f, dc * 128 * T + qoff + tc * 128,
                          [[T, 128], [1, 128]]) for dc in range(16)],
                     (lambda q_, tc_: lambda dc: ("spx1f", q_, tc_, dc))(q, tc)))

    def p4(q):
        qoff = q * TQ
        if q > 0:
            P.fence("dve", P4A_KEYS + P4B_KEYS, tag="p4a_fence")
        for dc in range(16):
            LD(x1f_q[dc][:, 0:TQ],
               dap(sp_x1f, dc * 128 * T + qoff, [[T, 128], [1, TQ]]),
               [("x1f", dc)],
               reads=[("spx1f", q, tc, dc) for tc in range(4)], q="pool",
               tag="x1ld")
        def wd_ld(ee2):
            for j in range(11):
                LD(wde[j][:, 0:D],
                   dap(s_wd, ee2 * FE * D + j * 128 * D, [[D, 128], [1, D]]),
                   [("wde", j)], reads=[("ffn_w",)], tag="wdld")
        wd_ld(0)
        for tc in range(4):
            bi = next_aux()
            for dc in range(16):
                P.add("pe", (lambda d_, t_, b: lambda e: e.matmul(
                    psx[b][:, 0:E], x1f_q[d_][:, t_ * 128:(t_ + 1) * 128],
                    wr_sb[:, d_ * E:(d_ + 1) * E], start=(d_ == 0), stop=(d_ == 15)))(
                        dc, tc, bi),
                    reads=[("x1f", dc), ("wr",), ("wde", 2)], writes=[kpx(bi)],
                    tag="rtmm")
            P.add("dve", (lambda b: lambda e: e.tensor_copy(
                rps[:, 0:E], psx[b][:, 0:E]))(bi),
                reads=[kpx(bi)], writes=[("rps",)], tag="rtev")
            P.add("dve", lambda e: e.reduce_max(small[0][:, 0:1], rps[:, 0:E],
                                                axis=AX.X),
                  reads=[("rps",)], writes=[("sm", 0)])
            P.add("dve", lambda e: e.tensor_scalar(
                small[0][:, 0:1], small[0][:, 0:1], -1.0, None, ALU.mult),
                reads=[("sm", 0)], writes=[("sm", 0)])
            P.add("act", lambda e: e.activation(
                rps[:, 8:16], rps[:, 0:E], AF.Exp, bias=small[0][:, 0:1]),
                reads=[("rps",), ("sm", 0)], writes=[("rps",)], tag="rtexp")
            P.add("dve", lambda e: e.reduce_sum(small[1][:, 0:1], rps[:, 8:16],
                                                axis=AX.X),
                  reads=[("rps",)], writes=[("sm", 1)])
            P.add("dve", lambda e: e.reciprocal(small[1][:, 0:1], small[1][:, 0:1]),
                  reads=[("sm", 1)], writes=[("sm", 1)])
            P.add("dve", lambda e: e.tensor_scalar(
                rps[:, 16:24], rps[:, 8:16], small[1][:, 0:1], None, ALU.mult),
                reads=[("rps",), ("sm", 1)], writes=[("rps",)], tag="rtp")
            P.add("dve", lambda e: e.reduce_max(small[2][:, 0:1], rps[:, 16:24],
                                                axis=AX.X),
                  reads=[("rps",)], writes=[("sm", 2)])
            P.add("dve", lambda e: e.tensor_scalar(
                rps[:, 24:32], rps[:, 16:24], small[2][:, 0:1], None, ALU.is_lt),
                reads=[("rps",), ("sm", 2)], writes=[("rps",)])
            P.add("dve", lambda e: e.tensor_mul(rps[:, 24:32], rps[:, 24:32],
                                                rps[:, 16:24]),
                  reads=[("rps",)], writes=[("rps",)])
            P.add("dve", lambda e: e.reduce_max(small[3][:, 0:1], rps[:, 24:32],
                                                axis=AX.X),
                  reads=[("rps",)], writes=[("sm", 3)])
            P.add("dve", lambda e: e.tensor_scalar(
                rps[:, 32:40], rps[:, 16:24], small[3][:, 0:1], None, ALU.is_ge),
                reads=[("rps",), ("sm", 3)], writes=[("rps",)])
            P.add("dve", (lambda t_: lambda e: e.tensor_mul(
                ctile[t_][:, 0:E], rps[:, 16:24], rps[:, 32:40]))(tc),
                reads=[("rps",)], writes=[("C", tc)], tag="rtC")
            if DBG_P3 and q == 0:
                ST(dap(dbg_C, tc * 128 * 8, [[8, 128], [1, 8]]),
                   ctile[tc][:, 0:E], [("dbgC", tc)], [("C", tc)],
                   q="pool", tag="dbgC")
                ST(dap(dbg_rps, tc * 128 * 40, [[40, 128], [1, 40]]),
                   rps[:, 0:40], [("dbgrps", tc)], [("rps",)],
                   q="pool", tag="dbgrps")
            P.add("pool", (lambda t_: lambda e: e.memset(
                moe_acc[t_][:, 0:D], 0.0))(tc),
                reads=(), writes=[("moe", tc)], tag="moez")
        for ee in range(E):
            if ee > 0:
                wd_ld(ee)

            def egu_ld(j2):
                wg2, wu2 = (2 * j2) % 4, (2 * j2 + 1) % 4
                LD(wring[wg2][:, 0:2048],
                   dap(s_wg, (ee * 11 + j2) * 128 * 2048, [[2048, 128], [1, 2048]]),
                   [("wring", wg2)], reads=[("ffn_w",)], tag="wgld")
                LD(wring[wu2][:, 0:2048],
                   dap(s_wu, (ee * 11 + j2) * 128 * 2048, [[2048, 128], [1, 2048]]),
                   [("wring", wu2)], reads=[("ffn_w",)], tag="wuld")
            egu_ld(0)
            for j in range(11):
                wgs, wus = (2 * j) % 4, (2 * j + 1) % 4
                if j + 1 < 11:
                    egu_ld(j + 1)
                    lag = [("wring", (2 * j + 2) % 4), ("wring", (2 * j + 3) % 4)]
                else:
                    LD(x1f_q[0][:, 0:TQ],
                       dap(sp_x1f, qoff, [[T, 128], [1, TQ]]),
                       [("x1f", 0)],
                       reads=[("spx1f", q, t2, 0) for t2 in range(4)], tag="xrf")
                    LD(x1f_q[1][:, 0:TQ],
                       dap(sp_x1f, 128 * T + qoff, [[T, 128], [1, TQ]]),
                       [("x1f", 1)],
                       reads=[("spx1f", q, t2, 1) for t2 in range(4)], tag="xrf")
                    lag = [("x1f", 0), ("x1f", 1)]
                bg = next_main()
                bu = next_main()
                for dc in range(16):
                    P.add("pe", (lambda w_, d_, b: lambda e: e.matmul(
                        psm[b][:, 0:TQ], wring[w_][:, d_ * 128:(d_ + 1) * 128],
                        x1f_q[d_][:, 0:TQ], start=(d_ == 0), stop=(d_ == 15)))(
                            wgs, dc, bg),
                        reads=[("wring", wgs), ("x1f", dc), ("wring", wus)] + lag,
                        writes=[kpm(bg)], tag="egmm")
                for dc in range(16):
                    P.add("pe", (lambda w_, d_, b: lambda e: e.matmul(
                        psm[b][:, 0:TQ], wring[w_][:, d_ * 128:(d_ + 1) * 128],
                        x1f_q[d_][:, 0:TQ], start=(d_ == 0), stop=(d_ == 15)))(
                            wus, dc, bu),
                        reads=[("wring", wus), ("x1f", dc)] + lag, writes=[kpm(bu)],
                        tag="eumm")
                P.add("act", (lambda b: lambda e: e.activation(
                    stg32[1][:, 0:TQ], psm[b][:, 0:TQ], AF.Silu))(bg),
                    reads=[kpm(bg)], writes=[("stg32", 1)], tag="esilu")
                P.add("dve", (lambda b, jj: lambda e: e.tensor_mul(
                    acte[jj][:, 0:TQ], stg32[1][:, 0:TQ], psm[b][:, 0:TQ]))(bu, j),
                    reads=[("stg32", 1), kpm(bu)], writes=[("acte", j)], tag="eact")
            for tc in range(4):
                for osub in range(4):
                    bi = next_main()
                    for j in range(11):
                        P.add("pe", (lambda jj, t_, o_, b: lambda e: e.matmul(
                            psm[b][:, 0:512], acte[jj][:, t_ * 128:(t_ + 1) * 128],
                            wde[jj][:, o_ * 512:(o_ + 1) * 512],
                            start=(jj == 0), stop=(jj == 10)))(j, tc, osub, bi),
                            reads=[("acte", j), ("wde", j), ("wde", 10)],
                            writes=[kpm(bi)], tag="edmm")
                    P.add("dve", (lambda t_, o_, b, e2: lambda e:
                          e.scalar_tensor_tensor(
                              moe_acc[t_][:, o_ * 512:(o_ + 1) * 512],
                              psm[b][:, 0:512], ctile[t_][:, e2:e2 + 1],
                              moe_acc[t_][:, o_ * 512:(o_ + 1) * 512],
                              ALU.mult, ALU.add))(tc, osub, bi, ee),
                        reads=[kpm(bi), ("C", tc), ("moe", tc)],
                        writes=[("moe", tc)], tag="emoe")
            if DBG_P3 and q == 0 and ee == E - 1:
                for tc in range(4):
                    ST(dap(dbg_moe, tc * 128 * D, [[D, 128], [1, D]]),
                       moe_acc[tc][:, 0:D], [("dbgmoe", tc)], [("moe", tc)],
                       q="pool", tag="dbgmoe")
        for tc in range(4):
            bi = next_aux()
            for dc in range(16):
                P.add("pe", (lambda d_, t_, b: lambda e: e.matmul(
                    psx[b][:, 0:1], x1f_q[d_][:, t_ * 128:(t_ + 1) * 128],
                    wsg_sb[:, d_:d_ + 1], start=(d_ == 0), stop=(d_ == 15)))(
                        dc, tc, bi),
                    reads=[("x1f", dc), ("wsg",)], writes=[kpx(bi)], tag="sgmm")
            P.add("act", (lambda t_, b: lambda e: e.activation(
                sigt[:, t_:t_ + 1], psx[b][:, 0:1], AF.Sigmoid))(tc, bi),
                reads=[kpx(bi)], writes=[("sig",)], tag="sgact")
        if DBG_P3 and q == 0:
            ST(dap(dbg_sig, 0, [[4, 128], [1, 4]]), sigt[:, 0:4],
               [("dbgsig",)], [("sig",)], q="pool", tag="dbgsig")
        P.fence("dve", P4B_KEYS + [("acte", j) for j in range(11)]
                + [("wde", j) for j in range(11)], tag="p4b_fence")
        for fg in range(11):
            def sgu_ld(j2):
                fc2 = fg * 4 + j2
                wg2, wu2 = (2 * j2) % 4, (2 * j2 + 1) % 4
                LD(wring[wg2][:, 0:2048],
                   dap(s_sg, fc2 * 128 * 2048, [[2048, 128], [1, 2048]]),
                   [("wring", wg2)], reads=[("ffn_w",)], tag="sgld")
                LD(wring[wu2][:, 0:2048],
                   dap(s_su, fc2 * 128 * 2048, [[2048, 128], [1, 2048]]),
                   [("wring", wu2)], reads=[("ffn_w",)], tag="suld")

            def sd_ld(j2):
                LD(wring[j2][:, 0:2048],
                   dap(s_sd, (fg * 4 + j2) * 128 * D, [[D, 128], [1, 2048]]),
                   [("wring", j2)], reads=[("ffn_w",)], tag="sdld")
            sgu_ld(0)
            for j in range(4):
                fc = fg * 4 + j
                wgs, wus = (2 * j) % 4, (2 * j + 1) % 4
                if j + 1 < 4:
                    sgu_ld(j + 1)
                    lag = [("wring", (2 * j + 2) % 4), ("wring", (2 * j + 3) % 4)]
                else:
                    sd_ld(0)
                    sd_ld(1)
                    lag = [("wring", 0), ("wring", 1)]
                bg = next_main()
                bu = next_main()
                for dc in range(16):
                    P.add("pe", (lambda w_, d_, b: lambda e: e.matmul(
                        psm[b][:, 0:TQ], wring[w_][:, d_ * 128:(d_ + 1) * 128],
                        x1f_q[d_][:, 0:TQ], start=(d_ == 0), stop=(d_ == 15)))(
                            wgs, dc, bg),
                        reads=[("wring", wgs), ("x1f", dc), ("wring", wus)] + lag,
                        writes=[kpm(bg)], tag="shg")
                for dc in range(16):
                    P.add("pe", (lambda w_, d_, b: lambda e: e.matmul(
                        psm[b][:, 0:TQ], wring[w_][:, d_ * 128:(d_ + 1) * 128],
                        x1f_q[d_][:, 0:TQ], start=(d_ == 0), stop=(d_ == 15)))(
                            wus, dc, bu),
                        reads=[("wring", wus), ("x1f", dc)] + lag, writes=[kpm(bu)],
                        tag="shu")
                P.add("act", (lambda b: lambda e: e.activation(
                    stg32[1][:, 0:TQ], psm[b][:, 0:TQ], AF.Silu))(bg),
                    reads=[kpm(bg)], writes=[("stg32", 1)], tag="ssilu")
                P.add("dve", (lambda b, jj: lambda e: e.tensor_mul(
                    actsh[jj][:, 0:TQ], stg32[1][:, 0:TQ], psm[b][:, 0:TQ]))(bu, j),
                    reads=[("stg32", 1), kpm(bu)], writes=[("actsh", j)], tag="sact")
            sd_ld(2)
            sd_ld(3)
            for tc in range(4):
                for osub in range(4):
                    bi = next_main()
                    for j in range(4):
                        P.add("pe", (lambda jj, t_, o_, b: lambda e: e.matmul(
                            psm[b][:, 0:512], actsh[jj][:, t_ * 128:(t_ + 1) * 128],
                            wring[jj][:, o_ * 512:(o_ + 1) * 512],
                            start=(jj == 0), stop=(jj == 3)))(j, tc, osub, bi),
                            reads=[("actsh", j), ("wring", j), ("wring", 3)],
                            writes=[kpm(bi)], tag="sdmm")
                    if fg == 0:
                        P.add("dve", (lambda t_, o_, b: lambda e: e.tensor_copy(
                            sh_acc[t_][:, o_ * 512:(o_ + 1) * 512],
                            psm[b][:, 0:512]))(tc, osub, bi),
                            reads=[kpm(bi)], writes=[("sh", tc, osub)], tag="sdcp")
                    else:
                        P.add("dve", (lambda t_, o_, b: lambda e: e.tensor_add(
                            sh_acc[t_][:, o_ * 512:(o_ + 1) * 512],
                            sh_acc[t_][:, o_ * 512:(o_ + 1) * 512],
                            psm[b][:, 0:512]))(tc, osub, bi),
                            reads=[kpm(bi), ("sh", tc, osub)],
                            writes=[("sh", tc, osub)], tag="sdadd")
            if DBG_P3 and q == 0 and fg == 10:
                for tc in range(4):
                    ST(dap(dbg_sh, tc * 128 * D, [[D, 128], [1, D]]),
                       sh_acc[tc][:, 0:D], [("dbgsh", tc)],
                       [("sh", tc, o) for o in range(4)], q="pool", tag="dbgsh")
        for tc in range(4):
            LD(h1t[:, 0:D], dap(sp_x1, (qoff + tc * 128) * D, [[D, 128], [1, D]]),
               [("h1t",)], reads=[("spx1", q, tc)], q="pool", tag="x1rld")
            LD(x1f_q[tc][:, 0:TQ],
               dap(sp_x1f, tc * 128 * T + qoff, [[T, 128], [1, TQ]]),
               [("x1f", tc)], reads=[("spx1f", q, t2, tc) for t2 in range(4)],
               q="pool", tag="xlag")
            LD(x1f_q[4 + tc][:, 0:TQ],
               dap(sp_x1f, (4 + tc) * 128 * T + qoff, [[T, 128], [1, TQ]]),
               [("x1f", 4 + tc)],
               reads=[("spx1f", q, t2, 4 + tc) for t2 in range(4)],
               q="pool", tag="xlag2")
            P.add("dve", (lambda t_: lambda e: e.scalar_tensor_tensor(
                ln0[:, 0:D], sh_acc[t_][:, 0:D], sigt[:, t_:t_ + 1],
                moe_acc[t_][:, 0:D], ALU.mult, ALU.add))(tc),
                reads=[("sig",), ("moe", tc)] + [("sh", tc, o) for o in range(4)],
                writes=[("ln0",)], tag="ffn")
            kpre = ("lnpre",)
            P.add("dve", (lambda t_: lambda e: e.tensor_add(
                ln0[:, 0:D], ln0[:, 0:D], h1t[:, 0:D]))(tc),
                reads=[("ln0",), ("h1t",), ("x1f", tc), ("x1f", 4 + tc)],
                writes=[kpre, ("ln0",)], tag="osum")
            ln_rows(kpre, dap(y_d, (qoff + tc * 128) * D, [[D, 128], [1, D]]),
                    ("out", q, tc))

    if phases is not None:
        p0a()
        if phases == "p3q0":
            p1a(0)
            p0b()
            p1b(0)
            p2(0)
            p0c()
            P.fence("dve", R1_KEYS + P3_KEYS, tag="p3fence")
            p3(0)
            P.add("sp", None, reads=[("x1", 0, tc, "sp32") if False else
                                     ("spx1", 0, tc) for tc in range(4)]
                  + [("spx1f", 0, tc, dc) for tc in range(4) for dc in range(16)],
                  tag="final_dbg")
            counts = P.emit(nc)
            print(f"[build-dbg] {counts} total={len(P.ops)}")
            nc._plan = P
            return nc
        if phases == "p1b0":
            p1a(0)
            p1b(0)
            P.add("sp", None, reads=[("spy2", 0, k) for k in range(32)],
                  tag="final_dbg")
            counts = P.emit(nc)
            print(f"[build-dbg] {counts} total={len(P.ops)}")
            nc._plan = P
            return nc
        if phases == "p1a0":
            p1a(0)
            keys = ([("spsz", 0, c, hh) for c in range(16) for hh in range(2)]
                    + [("spC", 0, n, hh) for n in range(16) for hh in range(2)]
                    + [("spdt", 0, c, hh) for c in range(16) for hh in range(2)]
                    + [("spdtx", 0, c, hh) for c in range(16) for hh in range(2)]
                    + [("spxe", 0, c, hh) for c in range(16) for hh in range(2)]
                    + [("spB", 0, n, hh) for n in range(16) for hh in range(2)])
            P.add("sp", None, reads=keys, tag="final_dbg")
        else:
            P.add("sp", None, reads=[("ipT", oc) for oc in range(NOC)]
                  + [("dtT", k) for k in range(32)], tag="final_dbg")
        counts = P.emit(nc)
        print(f"[build-dbg] {counts} total={len(P.ops)}")
        nc._plan = P
        return nc
    p0a()
    p1a(0)
    p1a(1)
    p0b()
    p1b(0)
    p1a(2)
    p2(0)
    p1b(1)
    p1a(3)
    p0c()
    p2(1)
    p1b(2)
    P.fence("dve", R1_KEYS + P3_KEYS, tag="p3fence")
    p3(0)
    p2(2)
    p1b(3)
    p3(1)
    p2(3)
    p0d()
    p3(2)
    p3(3)
    P.fence("dve", R1_KEYS + R2_KEYS + R3_KEYS + P3_KEYS + P4A_KEYS + P4B_KEYS
            + [("h1t",)], tag="p4fence")
    for q in range(NQ):
        p4(q)
    P.add("sp", None, reads=[("out", q, tc) for q in range(NQ)
                             for tc in range(4)], tag="final")

    counts = P.emit(nc)
    print(f"[build] ops per engine (n, waits): {counts} total={len(P.ops)}")
    nc._plan = P
    return nc


def kernel(**inputs):
    def g(name):
        return np.ascontiguousarray(np.asarray(inputs[name], dtype=np.float32))

    x = g("L0_attn_hidden_states")
    base = {
        "rms1_w": g("L0_attn_rms1_w"),
        "in_proj_w": g("L0_attn_in_proj_w"),
        "dt_proj_w": g("L0_attn_dt_proj_w"),
        "dt_proj_b": g("L0_attn_dt_proj_b"),
        "A_log": g("L0_attn_A_log"),
        "D_skip": g("L0_attn_D_skip"),
        "out_proj_w": g("L0_attn_out_proj_w"),
        "rms2_w": g("L0_attn_rms2_w"),
        "gate_w": g("L0_attn_gate_w"),
        "up_w": g("L0_attn_up_w"),
        "down_w": g("L0_attn_down_w"),
        "Wr": g("L0_ffn_Wr"),
        "Wg": g("L0_ffn_Wg"),
        "Wu": g("L0_ffn_Wu"),
        "Wd": g("L0_ffn_Wd"),
        "Sg": g("L0_ffn_Sg"),
        "Su": g("L0_ffn_Su"),
        "Sd": g("L0_ffn_Sd"),
        "Wsg": g("L0_ffn_Wsg"),
    }
    nc = build_program()
    in_maps = [dict(base, x=x[i]) for i in range(B_)]
    res = run_bass_kernel_spmd(nc, in_maps, core_ids=list(range(B_)))
    return np.stack([np.asarray(res.results[i]["y"]) for i in range(B_)], axis=0)
